# version 03
# baseline (speedup 1.0000x reference)
import numpy as np
from contextlib import ExitStack

import concourse.bass as bass
import concourse.mybir as mybir
from concourse.bass_utils import run_bass_kernel_spmd

F32 = mybir.dt.float32
BF16 = mybir.dt.bfloat16
I32 = mybir.dt.int32
AF = mybir.ActivationFunctionType
ALU = mybir.AluOpType
AX = mybir.AxisListType

NCORES = 8
T = 4096
D = 1024
NDC = D // 128

ENGS = ("sync", "vector", "scalar", "tensor", "gpsimd")

C_R127 = 0
C_NHALF = 1
C_EPS = 2
C_RD = 3
C_BIG = 4
C_ONE = 5


def make_cpool():
    cp = np.zeros((128, 8), np.float32)
    cp[:, C_R127] = 1.0 / 127.0
    cp[:, C_NHALF] = -0.5
    cp[:, C_EPS] = 1e-5
    cp[:, C_RD] = 1.0 / 1024.0
    cp[:, C_BIG] = -1e9
    cp[:, C_ONE] = 1.0
    return cp


class Plan:

    def __init__(self, nc, es):
        self.nc = nc
        self.es = es
        self.streams = {e: [] for e in ENGS}
        self._sid = 0
        self.prefix = ""
        self._scope_stack = []

    def begin_phase(self, prefix):
        self._scope_stack.append((self.es, self.prefix))
        self.es = ExitStack()
        self.prefix = prefix

    def end_phase(self):
        self.es.close()
        self.es, self.prefix = self._scope_stack.pop()

    def sem(self, name=None):
        h = self.es.enter_context(self.nc.semaphore(name or f"sem{self._sid}"))
        self._sid += 1
        return [h, 0]

    def tile(self, name, shape, dtype):
        return self.es.enter_context(
            self.nc.sbuf_tensor(self.prefix + name, list(shape), dtype))

    def psum(self, name, shape, dtype=F32):
        return self.es.enter_context(
            self.nc.psum_tensor(self.prefix + name, list(shape), dtype))

    def emit(self, eng, fn, inc=None, inck=1):
        if inc is not None:
            inc[1] += inck
        self.streams[eng].append((fn, None if inc is None else inc[0], inck))

    def op(self, eng, name, *a, inc=None, inck=1, **kw):
        def fn(e, name=name, a=a, kw=kw):
            return getattr(e, name)(*a, **kw)
        self.emit(eng, fn, inc, inck)

    def opd(self, eng, name, *a, inc=None, inck=1, **kw):
        self.op(eng, name, *a, inc=inc, inck=inck, **kw)
        self.op(eng, "drain")

    def wait(self, eng, sem, n=None):
        t = sem[1] if n is None else n
        h = sem[0]
        self.streams[eng].append((lambda e, h=h, t=t: e.wait_ge(h, t), None, 0))

    def mark(self, eng, sem):
        if eng in ("vector", "scalar", "gpsimd"):
            self.op(eng, "drain")
        self.op(eng, "nop", inc=sem)

    def dma(self, eng, out, in_, sem):
        self.op(eng, "dma_start", out=out, in_=in_, inc=sem, inck=16)

    def barrier(self, engs=ENGS):
        sems = getattr(self, "_bar_sems", None)
        if sems is None:
            sems = {e: self.sem(f"bar_{e}") for e in ENGS}
            self._bar_sems = sems
        for e in engs:
            if e in ("vector", "scalar", "gpsimd"):
                self.op(e, "drain")
            self.op(e, "nop", inc=sems[e])
        for e in engs:
            for o in engs:
                if o != e:
                    self.wait(e, sems[o])

    def finalize(self):
        blk = self.es.enter_context(self.nc.Block())
        for eng in ENGS:
            ops = self.streams[eng]
            if not ops:
                continue

            def body(e, ops=ops):
                for fn, semh, inck in ops:
                    r = fn(e)
                    if semh is not None:
                        r.then_inc(semh, inck)
            getattr(blk, eng)(body)


def stream(p, n_items, n_bufs, load_fn, use_fn, done_sem):
    marks = []
    for i in range(n_items):
        b = i % n_bufs
        if i >= n_bufs:
            p.wait("sync", done_sem, marks[i - n_bufs])
        load_fn(i, b)
        use_fn(i, b)
        marks.append(done_sem[1])



def emit_mirror(p, C, pairs):
    ld, vs, st = C["ld"], C["vs"], C["st"]
    NB = 3
    p.begin_phase(p.prefix + "m_")
    stg = [p.tile(f"mir_st{i}", (128, 4096), F32) for i in range(NB)]
    stb = [p.tile(f"mir_bf{i}", (128, 4096), BF16) for i in range(NB)]
    items = []
    for src, dst, rows, cols in pairs:
        for r0 in range(0, rows, 128):
            items.append((src, dst, r0, cols))
    marks = []
    for i, (src, dst, r0, cols) in enumerate(items):
        b = i % NB
        is_int = src.dtype == I32
        if i >= NB:
            p.wait("sync", vs, marks[i - NB][0])
        sview = stg[b][:, 0:cols].bitcast(I32) if is_int else stg[b][:, 0:cols]
        p.dma("sync", sview, src[r0:r0 + 128, :], ld)
        p.wait("vector", ld)
        if i >= NB:
            p.wait("vector", st, marks[i - NB][1])
        p.op("vector", "tensor_copy", out=stb[b][:, 0:cols], in_=sview, inc=vs)
        vmark = vs[1]
        p.wait("gpsimd", vs, vmark)
        p.op("gpsimd", "dma_start", out=dst[r0:r0 + 128, :], in_=stb[b][:, 0:cols],
             inc=st, inck=16)
        marks.append((vmark, st[1]))
    p.wait("sync", st)
    p.barrier()
    p.end_phase()


def emit_ln(p, x_chunks, tb, u, a_bc, b_bc):
    vs, ss, ts = u["vs"], u["ss"], u["ts"]
    nd = len(x_chunks)
    for c in range(nd):
        if c > 0:
            p.wait("scalar", ts)
        p.op("scalar", "activation", out=u["sq_tmp"][:, 0:tb], in_=x_chunks[c],
             func=AF.Square, inc=ss)
        p.wait("tensor", ss)
        p.op("tensor", "matmul", u["ps_s1"][:, 0:tb], u["ones"][:, 0:1], x_chunks[c],
             start=(c == 0), stop=(c == nd - 1))
        p.op("tensor", "matmul", u["ps_s2"][:, 0:tb], u["ones"][:, 0:1],
             u["sq_tmp"][:, 0:tb], start=(c == 0), stop=(c == nd - 1), inc=ts)
    p.wait("vector", ts)
    m_row, v_row, a_row, b_row = u["m_row"], u["v_row"], u["a_row"], u["b_row"]
    cp = u["cpool"]
    p.op("vector", "tensor_scalar_mul", out=m_row[:, 0:tb], in0=u["ps_s1"][:, 0:tb],
         scalar1=cp[0:1, C_RD:C_RD + 1], inc=vs)
    p.op("vector", "tensor_scalar_mul", out=v_row[:, 0:tb], in0=u["ps_s2"][:, 0:tb],
         scalar1=cp[0:1, C_RD:C_RD + 1], inc=vs)
    p.op("vector", "scalar_tensor_tensor", out=a_row[:, 0:tb], in0=m_row[:, 0:tb],
         scalar=1.0, in1=m_row[:, 0:tb], op0=ALU.mult, op1=ALU.mult, inc=vs)
    p.op("vector", "tensor_sub", out=v_row[:, 0:tb], in0=v_row[:, 0:tb],
         in1=a_row[:, 0:tb], inc=vs)
    p.wait("scalar", vs)
    p.op("scalar", "activation", out=v_row[:, 0:tb], in_=v_row[:, 0:tb],
         func=AF.Ln, bias=cp[0:1, C_EPS:C_EPS + 1], inc=ss)
    p.op("scalar", "activation", out=a_row[:, 0:tb], in_=v_row[:, 0:tb],
         func=AF.Exp, scale=cp[0:1, C_NHALF:C_NHALF + 1], inc=ss)
    p.wait("vector", ss)
    p.op("vector", "scalar_tensor_tensor", out=b_row[:, 0:tb], in0=m_row[:, 0:tb],
         scalar=-1.0, in1=a_row[:, 0:tb], op0=ALU.mult, op1=ALU.mult, inc=vs)
    p.wait("tensor", vs)
    p.op("tensor", "matmul", u["ps_bc"][:, 0:tb], u["ones"][0:1, 0:128],
         a_row[:, 0:tb], start=True, stop=True, inc=ts)
    p.wait("vector", ts)
    p.op("vector", "tensor_copy", out=a_bc[:, 0:tb], in_=u["ps_bc"][:, 0:tb], inc=vs)
    p.wait("tensor", vs)
    p.op("tensor", "matmul", u["ps_bc"][:, 0:tb], u["ones"][0:1, 0:128],
         b_row[:, 0:tb], start=True, stop=True, inc=ts)
    p.wait("vector", ts)
    p.op("vector", "tensor_copy", out=b_bc[:, 0:tb], in_=u["ps_bc"][:, 0:tb], inc=vs)


def alloc_ln_util(p, name="", alloc_ps=True, sq_buf=None, alloc_rows=True):
    u = {}
    u["sq_tmp"] = sq_buf if sq_buf is not None else p.tile(f"sq_tmp{name}", (128, 512), F32)
    if alloc_rows:
        u["m_row"] = p.tile(f"m_row{name}", (1, 512), F32)
        u["v_row"] = p.tile(f"v_row{name}", (1, 512), F32)
        u["a_row"] = p.tile(f"a_row{name}", (1, 512), F32)
        u["b_row"] = p.tile(f"b_row{name}", (1, 512), F32)
    if alloc_ps:
        u["ps_s1"] = p.psum(f"ps_s1{name}", (1, 512))
        u["ps_s2"] = p.psum(f"ps_s2{name}", (1, 512))
    return u



def emit_l1ffn(p, C, xT, wgT, wuT, wdT, wmir, oT):
    H = 3072
    NH = H // 128
    TB = 1024
    NTB = T // TB
    wgs, wus, wds = wmir
    emit_mirror(p, C, [
        (wgT, wgs, D, H),
        (wuT, wus, D, H),
        (wdT, wds, H, D),
    ])
    if True:
        ld = C["ld"]
        st = C["st"]
        vs = C["vs"]
        ss = C["ss"]
        ts = C["ts"]
        wdone = C["wdone"]
        cpool = C["cpool"]
        ones = C["ones"]
        x_sb = p.tile("x_sb", (128, NDC * TB), F32)
        xln = p.tile("xln", (128, NDC * TB), BF16)
        h_bf = p.tile("h_bf", (128, NH * 512), BF16)
        sg = p.tile("sg", (128, 512), F32)
        res = p.tile("res", (128, 512), F32)
        a_bc = p.tile("a_bc", (128, 512), F32)
        b_bc = p.tile("b_bc", (128, 512), F32)
        tmp_f = p.tile("tmp_f", (128, 512), F32)
        wbf = [p.tile(f"wbf{i}", (128, 8 * 512), BF16) for i in range(2)]
        wbf_u = [p.tile(f"wbfu{i}", (128, 8 * 512), BF16) for i in range(2)]
        wbf_d = [p.tile(f"wbfd{i}", (128, 1024), BF16) for i in range(3)]

        ps8 = [p.psum(f"f4ps{i}", (128, 512)) for i in range(8)]
        u = alloc_ln_util(p, alloc_ps=False)
        u.update(vs=vs, ss=ss, ts=ts, cpool=cpool, ones=ones)
        u["ps_bc"] = ps8[7]
        u["ps_s1"] = ps8[0][0:1, 0:512]
        u["ps_s2"] = ps8[1][0:1, 0:512]

        for tbi in range(NTB):
            t0 = tbi * TB
            for c in range(NDC):
                p.dma("sync", x_sb[:, c * TB:(c + 1) * TB],
                      xT[c * 128:(c + 1) * 128, t0:t0 + TB], ld)
            for e in ("vector", "scalar", "tensor"):
                p.wait(e, ld)

            for sub in range(TB // 512):
                o = sub * 512
                xch = [x_sb[:, c * TB + o:c * TB + o + 512] for c in range(NDC)]
                emit_ln(p, xch, 512, u, a_bc, b_bc)
                for c in range(NDC):
                    p.op("vector", "tensor_mul", out=tmp_f[:], in0=xch[c],
                         in1=a_bc[:], inc=vs)
                    p.op("vector", "tensor_add",
                         out=xln[:, c * TB + o:c * TB + o + 512],
                         in0=tmp_f[:], in1=b_bc[:], inc=vs)
                p.barrier()

                ev_marks = {}
                def load_gu(i, b):
                    hcg = i
                    for kk in range(NDC):
                        p.dma("sync", wbf[b][:, kk * 512:(kk + 1) * 512],
                              wgs[kk * 128:(kk + 1) * 128,
                                  hcg * 512:(hcg + 1) * 512], ld)
                        p.dma("sync", wbf_u[b][:, kk * 512:(kk + 1) * 512],
                              wus[kk * 128:(kk + 1) * 128,
                                  hcg * 512:(hcg + 1) * 512], ld)

                def use_gu(i, b):
                    hcg = i
                    p.wait("tensor", ld)
                    for m in range(4):
                        hc = hcg * 4 + m
                        for half, wb_ in ((0, wbf[b]), (1, wbf_u[b])):
                            bank = half * 4 + m
                            p.wait("tensor", ss, ev_marks.get(bank, 0))
                            p.wait("tensor", vs, ev_marks.get((bank, "v"), 0))
                            for kk in range(NDC):
                                p.op("tensor", "matmul", ps8[bank][:],
                                     wb_[:, kk * 512 + m * 128:kk * 512 + (m + 1) * 128],
                                     xln[:, kk * TB + o:kk * TB + o + 512],
                                     start=(kk == 0), stop=(kk == NDC - 1),
                                     inc=(ts if kk == NDC - 1 else None))
                        p.wait("scalar", ts)
                        p.wait("scalar", vs)
                        p.op("scalar", "activation", out=sg[:], in_=ps8[m][:],
                             func=AF.Silu, scale=cpool[:, C_R127:C_R127 + 1], inc=ss)
                        ev_marks[m] = ss[1]
                        p.wait("vector", ss)
                        p.op("vector", "scalar_tensor_tensor",
                             out=h_bf[:, hc * 512:(hc + 1) * 512],
                             in0=ps8[4 + m][:], scalar=cpool[:, C_R127:C_R127 + 1],
                             in1=sg[:], op0=ALU.mult, op1=ALU.mult, inc=vs)
                        ev_marks[(4 + m, "v")] = vs[1]
                    p.mark("vector", wdone)

                stream(p, 6, 2, load_gu, use_gu, wdone)
                p.barrier()

                def load_d(i, b):
                    kk = i
                    p.dma("sync", wbf_d[b][:],
                          wds[kk * 128:(kk + 1) * 128, :], ld)

                def use_d(i, b):
                    kk = i
                    p.wait("tensor", ld)
                    for cc in range(NDC):
                        inc2 = wdone if cc == NDC - 1 else (
                            ts if kk == NH - 1 else None)
                        p.op("tensor", "matmul", ps8[cc][:],
                             wbf_d[b][:, cc * 128:(cc + 1) * 128],
                             h_bf[:, kk * 512:(kk + 1) * 512],
                             start=(kk == 0), stop=(kk == NH - 1),
                             inc=inc2)

                stream(p, NH, 3, load_d, use_d, wdone)
                p.wait("vector", ts)
                p.wait("vector", wdone)
                for cc in range(NDC):
                    p.op("vector", "scalar_tensor_tensor", out=res[:],
                         in0=ps8[cc][:], scalar=cpool[:, C_R127:C_R127 + 1],
                         in1=x_sb[:, cc * TB + o:cc * TB + o + 512],
                         op0=ALU.mult, op1=ALU.add, inc=vs)
                    p.wait("gpsimd", vs)
                    p.op("gpsimd", "dma_start",
                         out=oT[cc * 128:(cc + 1) * 128, t0 + o:t0 + o + 512],
                         in_=res[:], inc=st, inck=16)
                    p.wait("vector", st)
                p.barrier()


def build_l1ffn():
    nc = bass.Bass("TRN2", target_bir_lowering=False)
    xT = nc.dram_tensor("xT", [D, T], F32, kind="ExternalInput").ap()
    wgT = nc.dram_tensor("f4_wgT", [D, 3072], I32, kind="ExternalInput").ap()
    wuT = nc.dram_tensor("f4_wuT", [D, 3072], I32, kind="ExternalInput").ap()
    wdT = nc.dram_tensor("f4_wdT", [3072, D], I32, kind="ExternalInput").ap()
    cpool_d = nc.dram_tensor("cpool", [128, 8], F32, kind="ExternalInput").ap()
    ones_d = nc.dram_tensor("ones", [128, 128], F32, kind="ExternalInput").ap()
    oT = nc.dram_tensor("oT", [D, T], F32, kind="ExternalOutput").ap()
    wgs = nc.dram_tensor("f4_wgs", [D, 3072], BF16, kind="Internal").ap()
    wus = nc.dram_tensor("f4_wus", [D, 3072], BF16, kind="Internal").ap()
    wds = nc.dram_tensor("f4_wds", [3072, D], BF16, kind="Internal").ap()
    es = ExitStack()
    with es:
        p = Plan(nc, es)
        C = make_ctx(p)
        p.dma("sync", C["cpool"][:], cpool_d, C["ld"])
        p.dma("sync", C["ones"][:], ones_d, C["ld"])
        for e in ("vector", "scalar", "tensor"):
            p.wait(e, C["ld"])
        p.begin_phase("f4_")
        emit_l1ffn(p, C, xT, wgT, wuT, wdT, (wgs, wus, wds), oT)
        p.end_phase()
        p.wait("gpsimd", C["st"])
        p.wait("sync", C["st"])
        p.finalize()
    return nc


def make_ctx(p):
    C = {}
    for nm in ("ld", "ld2", "st", "vs", "ss", "ts", "wdone"):
        C[nm] = p.sem(nm)
    C["cpool"] = p.tile("cpool_sb", (128, 8), F32)
    C["ones"] = p.tile("ones_sb", (128, 128), F32)
    C["ident"] = p.tile("ident_sb", (128, 128), F32)
    return C


def prep_l1ffn_inputs(x_fm, w_gate, w_up, w_down):
    return {
        "xT": np.ascontiguousarray(x_fm),
        "f4_wgT": np.ascontiguousarray(w_gate.T),
        "f4_wuT": np.ascontiguousarray(w_up.T),
        "f4_wdT": np.ascontiguousarray(w_down.T),
        "cpool": make_cpool(),
        "ones": np.ones((128, 128), np.float32),
    }



def emit_l0attn(p, C, xT, wT, ld0, diff_d, tri_d, oT):
    TB = 512
    NTB = T // TB
    if True:
        ld = C["ld"]
        st = C["st"]
        vs = C["vs"]
        ss = C["ss"]
        ts = C["ts"]
        wdone = C["wdone"]
        cpool = C["cpool"]
        ones = C["ones"]
        diffBD = p.tile("diff_sb", (128, 128), F32)
        triBD = p.tile("tri_sb", (128, 128), F32)
        mask = p.tile("mask", (128, 128), F32)
        ldt = p.tile("ldt", (1, 1), F32)
        scol = p.tile("scol", (128, 1), F32)
        wbf = {nm: p.tile(nm + "_bf", (128, NDC * D), BF16) for nm in wT}
        wstage = [p.tile(f"wst{i}", (128, D), F32) for i in range(2)]
        x_sb = p.tile("x_sb", (128, NDC * TB), F32)
        base = p.tile("base", (128, NDC * TB), F32)
        xn_bf = p.tile("xn_bf", (128, NDC * TB), BF16)
        q_bf = p.tile("q_bf", (128, NDC * TB), BF16)
        k_bf = p.tile("k_bf", (128, NDC * TB), BF16)
        v_bf = p.tile("v_bf", (128, 4 * D), BF16)
        g_f = p.tile("g_f", (128, NDC * TB), F32)
        go_bf = p.tile("go_bf", (128, NDC * TB), BF16)
        sm_bf = p.tile("sm_bf", (128, 128), BF16)
        a_bc = p.tile("a_bc", (128, 512), F32)
        b_bc = p.tile("b_bc", (128, 512), F32)
        s_bc = p.tile("s_bc", (128, 512), F32)
        tmp_f = p.tile("tmp_f", (128, 512), F32)
        res = p.tile("res", (128, 512), F32)

        u = alloc_ln_util(p)
        u.update(vs=vs, ss=ss, ts=ts, cpool=cpool, ones=ones)
        u["ps_bc"] = p.psum("ps_bc", (128, 512))
        ps_p = [p.psum("ps_p0", (128, 512)), p.psum("ps_p1", (128, 512))]
        ps_s = p.psum("ps_s", (128, 128))
        ps_o = [p.psum("ps_o0", (128, 128)), p.psum("ps_o1", (128, 128))]

        for t_, d_ in ((diffBD, diff_d), (triBD, tri_d), (ldt, ld0)):
            p.dma("sync", t_[:], d_, ld)
        for e in ("vector", "scalar", "tensor"):
            p.wait(e, ld)

        p.opd("scalar", "activation", out=ldt[:], in_=ldt[:], func=AF.Sigmoid, inc=ss)
        p.opd("scalar", "activation", out=ldt[:], in_=ldt[:], func=AF.Ln, inc=ss)
        p.wait("tensor", ss)
        p.op("tensor", "matmul", u["ps_bc"][:, 0:1], ones[0:1, 0:128], ldt[:],
             start=True, stop=True, inc=ts)
        p.wait("vector", ts)
        p.op("vector", "tensor_copy", out=scol[:], in_=u["ps_bc"][:, 0:1], inc=vs)
        p.wait("scalar", vs)
        p.op("scalar", "activation", out=mask[:], in_=diffBD[:], func=AF.Exp,
             scale=scol[:], inc=ss)
        p.wait("vector", ss)
        p.op("vector", "tensor_mul", out=mask[:], in0=mask[:], in1=triBD[:], inc=vs)

        def load_w(i, b):
            nm = list(wT)[i // NDC]
            c = i % NDC
            p.dma("sync", wstage[b][:], wT[nm][c * 128:(c + 1) * 128, :], ld)

        def use_w(i, b):
            nm = list(wT)[i // NDC]
            c = i % NDC
            p.wait("vector", ld)
            p.op("vector", "tensor_copy", out=wbf[nm][:, c * D:(c + 1) * D],
                 in_=wstage[b][:], inc=vs)
            p.mark("vector", wdone)

        stream(p, 5 * NDC, 2, load_w, use_w, wdone)
        p.barrier()

        for tbi in range(NTB):
            t0 = tbi * TB
            for c in range(NDC):
                p.dma("sync", x_sb[:, c * TB:(c + 1) * TB],
                      xT[c * 128:(c + 1) * 128, t0:t0 + TB], ld)
            for e in ("vector", "scalar", "tensor"):
                p.wait(e, ld)

            xch = [x_sb[:, c * TB:(c + 1) * TB] for c in range(NDC)]
            emit_ln(p, xch, TB, u, a_bc, b_bc)
            p.op("vector", "tensor_scalar_add", out=s_bc[:], in0=a_bc[:],
                 scalar1=cpool[:, C_ONE:C_ONE + 1], inc=vs)
            for c in range(NDC):
                p.op("vector", "tensor_mul", out=tmp_f[:], in0=xch[c], in1=a_bc[:], inc=vs)
                p.op("vector", "tensor_add", out=xn_bf[:, c * TB:(c + 1) * TB],
                     in0=tmp_f[:], in1=b_bc[:], inc=vs)
                p.op("vector", "tensor_mul", out=tmp_f[:], in0=xch[c], in1=s_bc[:], inc=vs)
                p.op("vector", "tensor_add", out=base[:, c * TB:(c + 1) * TB],
                     in0=tmp_f[:], in1=b_bc[:], inc=vs)
            p.barrier(("vector", "scalar", "tensor"))

            for wi, nm in enumerate(("wq", "wk", "wg")):
                for oc in range(NDC):
                    pp = ps_p[oc % 2]
                    for kk in range(NDC):
                        p.op("tensor", "matmul", pp[:],
                             wbf[nm][:, kk * D + oc * 128:kk * D + (oc + 1) * 128],
                             xn_bf[:, kk * TB:(kk + 1) * TB],
                             start=(kk == 0), stop=(kk == NDC - 1),
                             inc=(ts if kk == NDC - 1 else None))
                    p.wait("scalar", ts)
                    if nm == "wg":
                        p.op("scalar", "activation", out=g_f[:, oc * TB:(oc + 1) * TB],
                             in_=pp[:], func=AF.Sigmoid, inc=ss)
                    else:
                        dst = q_bf if nm == "wq" else k_bf
                        p.op("scalar", "copy", out=dst[:, oc * TB:(oc + 1) * TB],
                             in_=pp[:], inc=ss)
                    p.wait("tensor", ss)

            for sub in range(4):
                for half in range(2):
                    pp = ps_p[half]
                    for kk in range(NDC):
                        p.op("tensor", "matmul", pp[:],
                             xn_bf[:, kk * TB + sub * 128:kk * TB + (sub + 1) * 128],
                             wbf["wv"][:, kk * D + half * 512:kk * D + (half + 1) * 512],
                             start=(kk == 0), stop=(kk == NDC - 1),
                             inc=(ts if kk == NDC - 1 else None))
                    p.wait("scalar", ts)
                    p.op("scalar", "copy",
                         out=v_bf[:, sub * D + half * 512:sub * D + (half + 1) * 512],
                         in_=pp[:], inc=ss)
                    p.wait("tensor", ss)

            p.wait("vector", ss)
            for sub in range(4):
                sl = slice(sub * 128, (sub + 1) * 128)
                for kk in range(NDC):
                    p.op("tensor", "matmul", ps_s[:],
                         k_bf[:, kk * TB + sub * 128:kk * TB + (sub + 1) * 128],
                         q_bf[:, kk * TB + sub * 128:kk * TB + (sub + 1) * 128],
                         start=(kk == 0), stop=(kk == NDC - 1),
                         inc=(ts if kk == NDC - 1 else None))
                p.wait("vector", ts)
                p.op("vector", "tensor_mul", out=sm_bf[:], in0=ps_s[:], in1=mask[:], inc=vs)
                p.wait("tensor", vs)
                for dv in range(NDC):
                    po = ps_o[dv % 2]
                    p.op("tensor", "matmul", po[:],
                         v_bf[:, sub * D + dv * 128:sub * D + (dv + 1) * 128],
                         sm_bf[:], start=True, stop=True, inc=ts)
                    p.wait("vector", ts)
                    p.op("vector", "tensor_mul",
                         out=go_bf[:, dv * TB + sub * 128:dv * TB + (sub + 1) * 128],
                         in0=po[:], in1=g_f[:, dv * TB + sub * 128:dv * TB + (sub + 1) * 128],
                         inc=vs)
                    p.wait("tensor", vs)

            for oc in range(NDC):
                pp = ps_p[oc % 2]
                for kk in range(NDC):
                    p.op("tensor", "matmul", pp[:],
                         wbf["wo"][:, kk * D + oc * 128:kk * D + (oc + 1) * 128],
                         go_bf[:, kk * TB:(kk + 1) * TB],
                         start=(kk == 0), stop=(kk == NDC - 1),
                         inc=(ts if kk == NDC - 1 else None))
                p.wait("vector", ts)
                p.op("vector", "tensor_add", out=res[:], in0=pp[:],
                     in1=base[:, oc * TB:(oc + 1) * TB], inc=vs)
                p.wait("gpsimd", vs)
                p.op("gpsimd", "dma_start",
                     out=oT[oc * 128:(oc + 1) * 128, t0:t0 + TB],
                     in_=res[:], inc=st, inck=16)
                p.wait("vector", st)
                p.wait("tensor", vs)
            p.barrier()


def build_l0attn():
    nc = bass.Bass("TRN2", target_bir_lowering=False)
    xT = nc.dram_tensor("xT", [D, T], F32, kind="ExternalInput").ap()
    wT = {}
    for nm in ("wq", "wk", "wv", "wg", "wo"):
        wT[nm] = nc.dram_tensor("a1_" + nm + "T", [D, D], F32, kind="ExternalInput").ap()
    ld0 = nc.dram_tensor("a1_ld0", [1, 1], F32, kind="ExternalInput").ap()
    diff_d = nc.dram_tensor("a1_diffBD", [128, 128], F32, kind="ExternalInput").ap()
    tri_d = nc.dram_tensor("a1_triBD", [128, 128], F32, kind="ExternalInput").ap()
    cpool_d = nc.dram_tensor("cpool", [128, 8], F32, kind="ExternalInput").ap()
    ones_d = nc.dram_tensor("ones", [128, 128], F32, kind="ExternalInput").ap()
    oT = nc.dram_tensor("oT", [D, T], F32, kind="ExternalOutput").ap()
    es = ExitStack()
    with es:
        p = Plan(nc, es)
        C = make_ctx(p)
        p.dma("sync", C["cpool"][:], cpool_d, C["ld"])
        p.dma("sync", C["ones"][:], ones_d, C["ld"])
        for e in ("vector", "scalar", "tensor"):
            p.wait(e, C["ld"])
        p.begin_phase("a1_")
        emit_l0attn(p, C, xT, wT, ld0, diff_d, tri_d, oT)
        p.end_phase()
        p.wait("gpsimd", C["st"])
        p.wait("sync", C["st"])
        p.finalize()
    return nc


def prep_l0attn_inputs(x_fm, Wq, Wk, Wv, Wg, Wo, log_decay):
    pos = np.arange(32)
    diff32 = pos[:, None] - pos[None, :]
    tri32 = (diff32 >= 0).astype(np.float32)
    diffBD = np.zeros((128, 128), np.float32)
    triBD = np.zeros((128, 128), np.float32)
    for c in range(4):
        diffBD[c * 32:(c + 1) * 32, c * 32:(c + 1) * 32] = diff32.T
        triBD[c * 32:(c + 1) * 32, c * 32:(c + 1) * 32] = tri32.T
    return {
        "xT": np.ascontiguousarray(x_fm),
        "a1_wqT": np.ascontiguousarray(Wq.T), "a1_wkT": np.ascontiguousarray(Wk.T),
        "a1_wvT": np.ascontiguousarray(Wv.T), "a1_wgT": np.ascontiguousarray(Wg.T),
        "a1_woT": np.ascontiguousarray(Wo.T),
        "a1_ld0": np.asarray(log_decay[0:1], np.float32).reshape(1, 1),
        "cpool": make_cpool(),
        "ones": np.ones((128, 128), np.float32),
        "a1_diffBD": diffBD, "a1_triBD": triBD,
    }



def emit_l0ffn(p, C, xT, W1, W2, b1c, b2_d, wg_d, bg_d, sel8_d, scr, oT):
    TB = 1024
    NTB = T // TB
    NE = 8
    NHC = 32

    w1s, w2s = scr
    emit_mirror(p, C, [
        (W1.rearrange("e d h -> (e d) h"), w1s, 8 * D, 4096),
        (W2.rearrange("e h c -> (e h) c"), w2s, 8 * 4096, D),
    ])
    if True:
        ld = C["ld"]
        st = C["st"]
        vs = C["vs"]
        ss = C["ss"]
        ts = C["ts"]
        wdone = C["wdone"]
        cpool = C["cpool"]
        ones = C["ones"]
        ident = C["ident"]
        sel8 = p.tile("sel8_sb", (8, 8 * 128), F32)
        b1cols = p.tile("b1cols", (128, 256), F32)
        b2_sb = p.tile("b2_sb", (8, D), F32)
        b2_bf = p.tile("b2_bf", (8, D), BF16)
        wg_bf = p.tile("wg_bf", (128, NDC * 8), BF16)
        bg_bf = p.tile("bg_bf", (1, 8), BF16)
        onerow_bf = p.tile("onerow_bf", (1, 512), BF16)

        x_stage = p.tile("x_stage", (128, NDC * 512), F32)
        xln = p.tile("xln", (128, NDC * TB), BF16)
        y_sb = p.tile("y_sb", (128, NDC * TB), F32)
        h_bf = p.tile("h_bf", (128, NHC * TB), BF16)
        hr = [p.tile(f"hr{i}", (128, 512), F32) for i in range(2)]
        a_bc = p.tile("a_bc", (128, 512), F32)
        b_bc = p.tile("b_bc", (128, 512), F32)
        tmp_f = p.tile("tmp_f", (128, 512), F32)
        res = tmp_f
        w1bf = [p.tile(f"w1bf{i}", (128, 8 * 512), BF16) for i in range(2)]
        w2bf = [p.tile(f"w2bf{i}", (128, 1024), BF16) for i in range(3)]
        wbc = p.tile("wbc", (128, 2 * 512), F32)
        e_sb = p.tile("e_sb", (8, 512), F32)
        g_sb = p.tile("g_sb", (8, 512), F32)
        den_sb = p.tile("den_sb", (1, 512), F32)
        r_row = p.tile("r_row", (1, 512), F32)
        w_rows = p.tile("w_rows", (8, TB), F32)
        w_rows_bf = p.tile("w_rows_bf", (8, TB), BF16)
        gt = p.tile("gt", (128, 8), F32)
        sel1 = p.tile("sel1", (128, 8), F32)
        sel2 = p.tile("sel2", (128, 8), F32)
        g2 = p.tile("g2", (128, 8), F32)
        m1 = p.tile("m1", (128, 1), F32)
        m2 = p.tile("m2", (128, 1), F32)
        e1 = p.tile("e1", (128, 1), F32)
        e2 = p.tile("e2", (128, 1), F32)
        r12 = p.tile("r12", (128, 1), F32)
        found = p.tile("found", (128, 1), F32)
        nf = p.tile("nf", (128, 1), F32)
        colt = p.tile("colt", (128, 1), F32)
        zcol = p.tile("zcol", (128, 1), F32)
        onecol = p.tile("onecol", (128, 1), F32)
        wtok = p.tile("wtok", (128, 8), F32)

        u = alloc_ln_util(p, alloc_ps=False)
        u.update(vs=vs, ss=ss, ts=ts, cpool=cpool, ones=ones)
        ps8 = [p.psum(f"ps8_{i}", (128, 512)) for i in range(8)]
        ps_h = ps8
        ps_eo = ps8
        u["ps_bc"] = ps8[7]
        u["ps_s1"] = ps8[0][0:1, 0:512]
        u["ps_s2"] = ps8[1][0:1, 0:512]
        ps_l = ps_h[0][0:8, 0:512]
        ps_t = ps_eo[0][:, 0:128]

        for t_, d_ in ((sel8, sel8_d), (b1cols, b1c), (b2_sb, b2_d)):
            p.dma("sync", t_[:], d_, ld)
        bgf = p.tile("bgf", (1, 8), F32)
        p.dma("sync", bgf[:], bg_d, ld)
        wg_f = x_stage[:, 0:NDC * 8]
        for c in range(NDC):
            p.dma("sync", wg_f[:, c * 8:(c + 1) * 8],
                  wg_d[c * 128:(c + 1) * 128, :], ld)
        for e in ("vector", "scalar", "tensor"):
            p.wait(e, ld)
        p.op("vector", "tensor_copy", out=wg_bf[:], in_=wg_f, inc=vs)
        p.op("vector", "tensor_copy", out=bg_bf[:], in_=bgf[:], inc=vs)
        p.op("vector", "tensor_copy", out=b2_bf[:], in_=b2_sb[:], inc=vs)
        p.op("vector", "memset", onerow_bf[:], 1.0, inc=vs)
        p.op("vector", "memset", zcol[:], 0.0, inc=vs)
        p.op("vector", "memset", onecol[:], 1.0, inc=vs)
        p.barrier()

        for tbi in range(NTB):
            t0 = tbi * TB
            for sub in range(2):
                o = sub * 512
                for c in range(NDC):
                    p.dma("sync", x_stage[:, c * 512:(c + 1) * 512],
                          xT[c * 128:(c + 1) * 128, t0 + o:t0 + o + 512], ld)
                for e in ("vector", "scalar", "tensor"):
                    p.wait(e, ld)
                xch = [x_stage[:, c * 512:(c + 1) * 512] for c in range(NDC)]
                emit_ln(p, xch, 512, u, a_bc, b_bc)
                for c in range(NDC):
                    p.op("vector", "tensor_mul", out=tmp_f[:], in0=xch[c],
                         in1=a_bc[:], inc=vs)
                    p.op("vector", "tensor_add",
                         out=xln[:, c * TB + o:c * TB + o + 512],
                         in0=tmp_f[:], in1=b_bc[:], inc=vs)
                p.barrier(("vector", "scalar", "tensor", "sync"))

                for kk in range(NDC):
                    p.op("tensor", "matmul", ps_l,
                         wg_bf[:, kk * 8:(kk + 1) * 8],
                         xln[:, kk * TB + o:kk * TB + o + 512],
                         start=(kk == 0), stop=False)
                p.op("tensor", "matmul", ps_l, bg_bf[:], onerow_bf[:],
                     start=False, stop=True, inc=ts)
                p.wait("scalar", ts)
                p.op("scalar", "activation", out=e_sb[:], in_=ps_l,
                     func=AF.Exp, inc=ss)
                p.wait("tensor", ss)
                p.op("tensor", "matmul", u["ps_s1"][:, 0:512], ones[0:8, 0:1],
                     e_sb[:], start=True, stop=True, inc=ts)
                p.wait("vector", ts)
                p.opd("vector", "tensor_copy", out=den_sb[:], in_=u["ps_s1"][:, 0:512], inc=vs)
                p.opd("vector", "reciprocal", out=r_row[:], in_=den_sb[:], inc=vs)
                p.wait("tensor", vs)
                p.op("tensor", "matmul", u["ps_bc"][0:8, 0:512], ones[0:1, 0:8],
                     r_row[:], start=True, stop=True, inc=ts)
                p.wait("vector", ts)
                p.opd("vector", "tensor_mul", out=g_sb[:], in0=e_sb[:],
                     in1=u["ps_bc"][0:8, 0:512], inc=vs)
                for s4 in range(4):
                    p.wait("tensor", vs)
                    p.op("tensor", "transpose", ps_t[:, 0:8],
                         g_sb[:, s4 * 128:(s4 + 1) * 128], ident[0:8, 0:8], inc=ts)
                    p.wait("vector", ts)
                    p.opd("vector", "tensor_copy", out=gt[:], in_=ps_t[:, 0:8], inc=vs)
                    p.opd("vector", "tensor_reduce", out=m1[:], in_=gt[:],
                         axis=AX.X, op=ALU.max, inc=vs)
                    p.opd("vector", "memset", found[:], 0.0, inc=vs)
                    for e8 in range(8):
                        p.opd("vector", "tensor_tensor", out=colt[:],
                             in0=gt[:, e8:e8 + 1], in1=m1[:], op=ALU.is_ge, inc=vs)
                        p.opd("vector", "tensor_tensor", out=colt[:].bitcast(I32),
                             in0=colt[:].bitcast(I32), in1=onecol[:].bitcast(I32),
                             op=ALU.bitwise_and, inc=vs)
                        p.opd("vector", "tensor_scalar", out=nf[:], in0=found[:],
                             scalar1=-1.0, scalar2=1.0, op0=ALU.mult, op1=ALU.add, inc=vs)
                        p.opd("vector", "tensor_mul", out=sel1[:, e8:e8 + 1],
                             in0=colt[:], in1=nf[:], inc=vs)
                        p.opd("vector", "tensor_add", out=found[:], in0=found[:],
                             in1=sel1[:, e8:e8 + 1], inc=vs)
                    p.opd("vector", "scalar_tensor_tensor", out=g2[:], in0=sel1[:],
                         scalar=cpool[:, C_BIG:C_BIG + 1], in1=gt[:],
                         op0=ALU.mult, op1=ALU.add, inc=vs)
                    p.opd("vector", "tensor_reduce", out=m2[:], in_=g2[:],
                         axis=AX.X, op=ALU.max, inc=vs)
                    p.opd("vector", "memset", found[:], 0.0, inc=vs)
                    for e8 in range(8):
                        p.opd("vector", "tensor_tensor", out=colt[:],
                             in0=g2[:, e8:e8 + 1], in1=m2[:], op=ALU.is_ge, inc=vs)
                        p.opd("vector", "tensor_tensor", out=colt[:].bitcast(I32),
                             in0=colt[:].bitcast(I32), in1=onecol[:].bitcast(I32),
                             op=ALU.bitwise_and, inc=vs)
                        p.opd("vector", "tensor_scalar", out=nf[:], in0=found[:],
                             scalar1=-1.0, scalar2=1.0, op0=ALU.mult, op1=ALU.add, inc=vs)
                        p.opd("vector", "tensor_mul", out=sel2[:, e8:e8 + 1],
                             in0=colt[:], in1=nf[:], inc=vs)
                        p.opd("vector", "tensor_add", out=found[:], in0=found[:],
                             in1=sel2[:, e8:e8 + 1], inc=vs)
                    p.wait("scalar", vs)
                    p.op("scalar", "activation", out=e1[:], in_=m1[:], func=AF.Exp, inc=ss)
                    p.op("scalar", "activation", out=e2[:], in_=m2[:], func=AF.Exp, inc=ss)
                    p.wait("vector", ss)
                    p.opd("vector", "tensor_add", out=r12[:], in0=e1[:], in1=e2[:], inc=vs)
                    p.opd("vector", "reciprocal", out=r12[:], in_=r12[:], inc=vs)
                    p.opd("vector", "tensor_mul", out=e1[:], in0=e1[:], in1=r12[:], inc=vs)
                    p.opd("vector", "tensor_mul", out=e2[:], in0=e2[:], in1=r12[:], inc=vs)
                    p.opd("vector", "tensor_scalar_mul", out=wtok[:], in0=sel1[:],
                         scalar1=e1[:], inc=vs)
                    p.opd("vector", "scalar_tensor_tensor", out=wtok[:], in0=sel2[:],
                         scalar=e2[:], in1=wtok[:], op0=ALU.mult, op1=ALU.add, inc=vs)
                    p.wait("tensor", vs)
                    p.op("tensor", "transpose", ps_t[0:8, 0:128], wtok[:],
                         ident[:], inc=ts)
                    p.wait("vector", ts)
                    p.opd("vector", "tensor_copy",
                         out=w_rows[:, o + s4 * 128:o + (s4 + 1) * 128],
                         in_=ps_t[0:8, 0:128], inc=vs)
                p.barrier(("vector", "scalar", "tensor"))
            p.opd("vector", "tensor_copy", out=w_rows_bf[:], in_=w_rows[:], inc=vs)
            p.opd("vector", "memset", y_sb[:], 0.0, inc=vs)
            p.barrier()

            for e8 in range(NE):
                for t5 in range(2):
                    p.wait("tensor", vs)
                    p.op("tensor", "matmul", ps8[7][:, 0:512],
                         sel8[:, e8 * 128:(e8 + 1) * 128],
                         w_rows[:, t5 * 512:(t5 + 1) * 512],
                         start=True, stop=True, inc=ts)
                    p.wait("vector", ts)
                    p.op("vector", "tensor_copy", out=wbc[:, t5 * 512:(t5 + 1) * 512],
                         in_=ps8[7][:, 0:512], inc=vs)
                p.barrier(("vector", "tensor", "scalar", "sync"))

                relu_marks = {}
                hr_marks = {}
                def load_w1(i, b):
                    hcg = i
                    for kk in range(NDC):
                        p.dma("sync", w1bf[b][:, kk * 512:(kk + 1) * 512],
                              w1s[e8 * D + kk * 128:e8 * D + (kk + 1) * 128,
                                  hcg * 512:(hcg + 1) * 512], ld)

                def use_w1(i, b):
                    hcg = i
                    p.wait("tensor", ld)
                    for t5 in range(2):
                        oo = t5 * 512
                        for m in range(4):
                            hc = hcg * 4 + m
                            bank = t5 * 4 + m
                            p.wait("tensor", ss, relu_marks.get(bank, 0))
                            for kk in range(NDC):
                                p.op("tensor", "matmul", ps8[bank][:],
                                     w1bf[b][:, kk * 512 + m * 128:kk * 512 + (m + 1) * 128],
                                     xln[:, kk * TB + oo:kk * TB + oo + 512],
                                     start=(kk == 0), stop=(kk == NDC - 1),
                                     inc=(ts if kk == NDC - 1 else None))
                            hrb = hr[hc % 2]
                            p.wait("scalar", ts)
                            p.wait("scalar", vs, hr_marks.get(hc % 2, 0))
                            p.op("scalar", "activation", out=hrb[:], in_=ps8[bank][:],
                                 func=AF.Relu,
                                 bias=b1cols[:, e8 * 32 + hc:e8 * 32 + hc + 1], inc=ss)
                            relu_marks[bank] = ss[1]
                            p.wait("vector", ss)
                            p.op("vector", "tensor_mul",
                                 out=h_bf[:, hc * TB + oo:hc * TB + oo + 512],
                                 in0=hrb[:], in1=wbc[:, oo:oo + 512], inc=vs)
                            hr_marks[hc % 2] = vs[1]
                    p.mark("vector", wdone)

                stream(p, NDC, 2, load_w1, use_w1, wdone)
                p.barrier(("vector", "scalar", "tensor", "sync"))

                for t5 in range(2):
                    oo = t5 * 512
                    def load_w2(i, b, t5=t5):
                        kk = i
                        p.dma("sync", w2bf[b][:],
                              w2s[e8 * 4096 + kk * 128:e8 * 4096 + (kk + 1) * 128, :], ld)

                    def use_w2(i, b, t5=t5, oo=oo):
                        kk = i
                        p.wait("tensor", ld)
                        for cc in range(NDC):
                            inc2 = wdone if cc == NDC - 1 else (
                                ts if kk == NHC - 1 else None)
                            p.op("tensor", "matmul", ps8[cc][:],
                                 w2bf[b][:, cc * 128:(cc + 1) * 128],
                                 h_bf[:, kk * TB + oo:kk * TB + oo + 512],
                                 start=(kk == 0), stop=(kk == NHC - 1),
                                 inc=inc2)

                    stream(p, NHC, 3, load_w2, use_w2, wdone)
                    p.wait("vector", ts)
                    p.wait("vector", wdone)
                    for cc in range(NDC):
                        p.op("vector", "tensor_add",
                             out=y_sb[:, cc * TB + oo:cc * TB + oo + 512],
                             in0=y_sb[:, cc * TB + oo:cc * TB + oo + 512],
                             in1=ps8[cc][:], inc=vs)
                    p.barrier(("vector", "tensor", "sync"))
                p.barrier(("vector", "scalar", "tensor", "sync"))

            for cc in range(NDC):
                for t5 in range(2):
                    oo = t5 * 512
                    p.wait("tensor", vs)
                    p.op("tensor", "matmul", ps_eo[t5][:],
                         b2_bf[:, cc * 128:(cc + 1) * 128],
                         w_rows_bf[:, oo:oo + 512], start=True, stop=True, inc=ts)
                    p.wait("sync", vs)
                    p.dma("sync", x_stage[:, 0:512],
                          xT[cc * 128:(cc + 1) * 128, t0 + oo:t0 + oo + 512], ld)
                    p.wait("vector", ts)
                    p.wait("vector", ld)
                    p.op("vector", "tensor_add", out=res[:],
                         in0=y_sb[:, cc * TB + oo:cc * TB + oo + 512],
                         in1=ps_eo[t5][:], inc=vs)
                    p.op("vector", "tensor_add", out=res[:], in0=res[:],
                         in1=x_stage[:, 0:512], inc=vs)
                    p.wait("gpsimd", vs)
                    p.op("gpsimd", "dma_start",
                         out=oT[cc * 128:(cc + 1) * 128, t0 + oo:t0 + oo + 512],
                         in_=res[:], inc=st, inck=16)
                    p.wait("vector", st)
            p.barrier()


def build_l0ffn():
    nc = bass.Bass("TRN2", target_bir_lowering=False)
    xT = nc.dram_tensor("xT", [D, T], F32, kind="ExternalInput").ap()
    W1 = nc.dram_tensor("a2_W1", [8, D, 4096], F32, kind="ExternalInput").ap()
    W2 = nc.dram_tensor("a2_W2", [8, 4096, D], F32, kind="ExternalInput").ap()
    b1c = nc.dram_tensor("a2_b1c", [128, 256], F32, kind="ExternalInput").ap()
    b2_d = nc.dram_tensor("a2_b2", [8, D], F32, kind="ExternalInput").ap()
    wg_d = nc.dram_tensor("a2_wg", [D, 8], F32, kind="ExternalInput").ap()
    bg_d = nc.dram_tensor("a2_bg", [1, 8], F32, kind="ExternalInput").ap()
    sel8_d = nc.dram_tensor("a2_sel8", [8, 8 * 128], F32, kind="ExternalInput").ap()
    cpool_d = nc.dram_tensor("cpool", [128, 8], F32, kind="ExternalInput").ap()
    ones_d = nc.dram_tensor("ones", [128, 128], F32, kind="ExternalInput").ap()
    ident_d = nc.dram_tensor("ident", [128, 128], F32, kind="ExternalInput").ap()
    oT = nc.dram_tensor("oT", [D, T], F32, kind="ExternalOutput").ap()
    w1s = nc.dram_tensor("a2_w1s", [8 * D, 4096], BF16, kind="Internal").ap()
    w2s = nc.dram_tensor("a2_w2s", [8 * 4096, D], BF16, kind="Internal").ap()
    es = ExitStack()
    with es:
        p = Plan(nc, es)
        C = make_ctx(p)
        p.dma("sync", C["cpool"][:], cpool_d, C["ld"])
        p.dma("sync", C["ones"][:], ones_d, C["ld"])
        p.dma("sync", C["ident"][:], ident_d, C["ld"])
        for e in ("vector", "scalar", "tensor"):
            p.wait(e, C["ld"])
        p.begin_phase("a2_")
        emit_l0ffn(p, C, xT, W1, W2, b1c, b2_d, wg_d, bg_d, sel8_d, (w1s, w2s), oT)
        p.end_phase()
        p.wait("gpsimd", C["st"])
        p.wait("sync", C["st"])
        p.finalize()
    return nc


def prep_l0ffn_inputs(x_fm, W1, b1, W2, b2, Wg, bg):
    b1c = np.ascontiguousarray(
        b1.reshape(8, 32, 128).transpose(2, 0, 1).reshape(128, 256))
    sel8 = np.zeros((8, 8 * 128), np.float32)
    for e in range(8):
        sel8[e, e * 128:(e + 1) * 128] = 1.0
    return {
        "xT": np.ascontiguousarray(x_fm),
        "a2_W1": np.ascontiguousarray(W1), "a2_W2": np.ascontiguousarray(W2),
        "a2_b1c": b1c, "a2_b2": np.ascontiguousarray(b2),
        "a2_wg": np.ascontiguousarray(Wg), "a2_bg": bg.reshape(1, 8).astype(np.float32),
        "cpool": make_cpool(),
        "ones": np.ones((128, 128), np.float32),
        "ident": np.eye(128, dtype=np.float32),
        "a2_sel8": sel8,
    }



def emit_l1attn(p, C, xT, WiT, WxT, WdtT, WoT, cwc_d, cbc_d, bdtc_d, dpc_d,
                alogc_d, lnwc_d, lnbc_d, sel16_d, scr, wmir, oT):
    DI = 2048
    NDI = DI // 128
    NL = 4
    TC = 512
    NCH = T // TC
    DTR = 64
    NS = 16
    z_scr, u_scr, dt_scr, w_scr = scr
    wis, wos = wmir
    emit_mirror(p, C, [
        (WiT.rearrange("l d j -> (l d) j"), wis, NL * D, 2 * DI),
        (WoT.rearrange("l d o -> (l d) o"), wos, NL * DI, D),
    ])
    if True:
        ld = C["ld"]
        ld2 = C["ld2"]
        st = C["st"]
        vs = C["vs"]
        ss = C["ss"]
        ts = C["ts"]
        wdone = C["wdone"]

        cpool = C["cpool"]
        ones = C["ones"]
        cwc = p.tile("cwc_sb", (128, NL * 64), F32)
        cbc = p.tile("cbc_sb", (128, NL * 16), F32)
        bdtc = p.tile("bdtc_sb", (128, NL * 16), F32)
        dpc = p.tile("dpc_sb", (128, NL * 16), F32)
        A_sb = p.tile("A_sb", (128, NL * 256), F32)
        lnwc = p.tile("lnwc_sb", (128, NL * 8), F32)
        lnbc = p.tile("lnbc_sb", (128, NL * 8), F32)

        x_cur = p.tile("x_cur", (128, NDC * TC), F32)
        xln = p.tile("xln", (128, NDC * TC), BF16)
        xs_t = [p.tile(f"xs_t{i}", (128, 515), F32) for i in range(2)]
        halo = p.tile("halo", (128, NL * NDI * 3), F32)
        hcarry = p.tile("hcarry", (128, NL * NDI * 16), F32)
        u_t = [p.tile(f"u_t{i}", (128, TC), F32) for i in range(4)]
        z_t = [p.tile(f"z_t{i}", (128, TC), F32) for i in range(4)]
        dt_t = [p.tile(f"dt_t{i}", (128, TC), F32) for i in range(4)]
        w_t = [p.tile(f"w_t{i}", (128, TC), F32) for i in range(4)]
        Bint = p.tile("Bint", (128, NS * TC), F32)
        Cint = p.tile("Cint", (128, NS * TC), F32)
        DA_bufs = [p.tile(f"DA{i}", (128, 2 * 513), F32) for i in range(2)]
        XH = [p.tile(f"XH{i}", (128, 2 * 513), F32) for i in range(2)]
        PQ0b = [p.tile(f"PQ0_{i}", (128, 512), F32) for i in range(2)]
        PQ1 = p.tile("PQ1", (128, 512), F32)
        y_dc = p.tile("y_dc", (128, TC), F32)
        y0 = p.tile("y0", (128, TC), F32)
        y2_bf = p.tile("y2_bf", (128, NDI * TC), BF16)
        dtraw_bf = p.tile("dtraw_bf", (64, TC), BF16)
        bc_rows = p.tile("bc_rows", (16, TC), F32)
        cc_rows = p.tile("cc_rows", (16, TC), F32)
        sel16 = p.tile("sel16_sb", (16, 16 * 128), F32)
        wi_bf = [p.tile(f"wi_bf{i}", (128, 512), BF16) for i in range(3)]
        wx_sb = p.tile("wx_sb", (128, NDI * 96), F32)
        wdt_bf = p.tile("wdt_bf", (64, DI), BF16)
        tmp_f = p.tile("tmp_f", (128, 512), F32)
        a_bc = DA_bufs[0][:, 0:512]
        b_bc = DA_bufs[1][:, 0:512]


        u = alloc_ln_util(p, alloc_ps=False, sq_buf=tmp_f, alloc_rows=False)
        u["m_row"] = DA_bufs[0][0:1, 513:1025]
        u["v_row"] = DA_bufs[1][0:1, 513:1025]
        u["a_row"] = XH[0][0:1, 0:512]
        u["b_row"] = XH[1][0:1, 0:512]
        u.update(vs=vs, ss=ss, ts=ts, cpool=cpool, ones=ones)
        u["ps_bc"] = p.psum("ps_bc", (128, 512))
        ps_m = [p.psum(f"ps_m{i}", (128, 512)) for i in range(4)]
        ps_xd = p.psum("ps_xd", (80, 512))
        ps_c16 = p.psum("ps_c16", (16, 512))
        u["ps_s1"] = ps_xd[0:1, 0:512]
        u["ps_s2"] = ps_c16[0:1, 0:512]

        p.dma("sync", sel16[:], sel16_d, ld)
        for l in range(NL):
            p.dma("sync", cwc[:, l * 64:(l + 1) * 64], cwc_d[l], ld)
            p.dma("sync", cbc[:, l * 16:(l + 1) * 16], cbc_d[l], ld)
            p.dma("sync", bdtc[:, l * 16:(l + 1) * 16], bdtc_d[l], ld)
            p.dma("sync", dpc[:, l * 16:(l + 1) * 16], dpc_d[l], ld)
            p.dma("sync", A_sb[:, l * 256:(l + 1) * 256], alogc_d[l], ld)
            p.dma("sync", lnwc[:, l * 8:(l + 1) * 8], lnwc_d[l], ld)
            p.dma("sync", lnbc[:, l * 8:(l + 1) * 8], lnbc_d[l], ld)
        for e in ("vector", "scalar", "tensor", "gpsimd"):
            p.wait(e, ld)
        p.op("scalar", "activation", out=A_sb[:], in_=A_sb[:], func=AF.Exp, inc=ss)
        p.wait("vector", ss)
        p.op("vector", "tensor_scalar_mul", out=A_sb[:], in0=A_sb[:], scalar1=-1.0, inc=vs)
        p.op("vector", "memset", hcarry[:], 0.0, inc=vs)
        p.op("vector", "memset", halo[:], 0.0, inc=vs)
        p.barrier()

        for ch in range(NCH):
            t0 = ch * TC
            for c in range(NDC):
                p.dma("sync", x_cur[:, c * TC:(c + 1) * TC],
                      xT[c * 128:(c + 1) * 128, t0:t0 + TC], ld)
            for e in ("vector", "scalar", "tensor"):
                p.wait(e, ld)
            xch = [x_cur[:, c * TC:(c + 1) * TC] for c in range(NDC)]
            emit_ln(p, xch, TC, u, a_bc, b_bc)
            for c in range(NDC):
                p.op("vector", "tensor_mul", out=tmp_f[:], in0=xch[c], in1=a_bc, inc=vs)
                p.op("vector", "tensor_add", out=xch[c], in0=tmp_f[:], in1=b_bc, inc=vs)
            p.barrier(("vector", "scalar", "tensor"))

            for l in range(NL):
                spill_marks = {}
                lnw = lnwc[:, l * 8:(l + 1) * 8]
                lnb = lnbc[:, l * 8:(l + 1) * 8]
                emit_ln(p, xch, TC, u, a_bc, b_bc)
                for c in range(NDC):
                    p.op("vector", "scalar_tensor_tensor", out=tmp_f[:], in0=xch[c],
                         scalar=lnw[:, c:c + 1], in1=a_bc[:],
                         op0=ALU.mult, op1=ALU.mult, inc=vs)
                    p.op("vector", "tensor_scalar", out=y_dc[:], in0=b_bc,
                         scalar1=lnw[:, c:c + 1], scalar2=lnb[:, c:c + 1],
                         op0=ALU.mult, op1=ALU.add, inc=vs)
                    p.op("vector", "tensor_add", out=xln[:, c * TC:(c + 1) * TC],
                         in0=tmp_f[:], in1=y_dc[:], inc=vs)
                p.barrier(("vector", "scalar", "tensor"))

                for dc in range(NDI):
                    p.dma("sync", wx_sb[:, dc * 96:(dc + 1) * 96],
                          WxT[l, dc * 128:(dc + 1) * 128, :], ld)
                wdt_f = Bint[0:64, 0:DI]
                p.dma("sync", wdt_f, WdtT[l], ld)
                p.wait("vector", ld)
                p.op("vector", "tensor_copy", out=wdt_bf[:], in_=wdt_f, inc=vs)
                p.wait("tensor", ld)
                p.wait("scalar", ld)

                for mg in range(8):
                    is_z = mg >= 4
                    def load_wi(i, b, mg=mg, l=l):
                        kk = i
                        p.dma("sync", wi_bf[b][:],
                              wis[l * D + kk * 128:l * D + (kk + 1) * 128,
                                  mg * 512:(mg + 1) * 512], ld)
                    def use_wi(i, b, mg=mg):
                        kk = i
                        p.wait("tensor", ld)
                        for m in range(4):
                            inc2 = wdone if m == 3 else (
                                ts if kk == NDC - 1 else None)
                            p.op("tensor", "matmul", ps_m[m][:],
                                 wi_bf[b][:, m * 128:(m + 1) * 128],
                                 xln[:, kk * TC:(kk + 1) * TC],
                                 start=(kk == 0), stop=(kk == NDC - 1),
                                 inc=inc2)
                    stream(p, NDC, 3, load_wi, use_wi, wdone)
                    p.wait("scalar", ts)
                    p.wait("scalar", wdone)
                    for m in range(4):
                        dc = (mg % 4) * 4 + m
                        if not is_z:
                            xst = xs_t[m % 2]
                            hsl = halo[:, (l * NDI + dc) * 3:(l * NDI + dc) * 3 + 3]
                            p.wait("vector", vs, spill_marks.get(("xs", m % 2), 0))
                            p.op("vector", "tensor_copy", out=xst[:, 0:3], in_=hsl, inc=vs)
                            p.wait("scalar", vs)
                            p.op("scalar", "copy", out=xst[:, 3:515], in_=ps_m[m][:], inc=ss)
                            p.wait("vector", ss)
                            p.op("vector", "tensor_copy", out=hsl, in_=xst[:, 512:515], inc=vs)
                            cw = cwc[:, l * 64:]
                            ub = u_t[dc % 4]
                            p.wait("vector", ts)
                            p.wait("vector", st, spill_marks.get(("u", dc % 4), 0))
                            p.op("vector", "tensor_scalar_mul", out=ub[:],
                                 in0=xst[:, 0:512],
                                 scalar1=cw[:, 0 * 16 + dc:0 * 16 + dc + 1], inc=vs)
                            for j in range(1, 4):
                                p.op("vector", "scalar_tensor_tensor", out=ub[:],
                                     in0=xst[:, j:512 + j],
                                     scalar=cw[:, j * 16 + dc:j * 16 + dc + 1],
                                     in1=ub[:], op0=ALU.mult, op1=ALU.add, inc=vs)
                            spill_marks[("xs", m % 2)] = vs[1]
                            p.wait("scalar", vs)
                            p.op("scalar", "activation", out=ub[:], in_=ub[:],
                                 func=AF.Silu,
                                 bias=cbc[:, l * 16 + dc:l * 16 + dc + 1], inc=ss)
                            p.wait("tensor", ss)
                            p.op("tensor", "matmul", ps_xd[:],
                                 wx_sb[:, dc * 96:dc * 96 + 80],
                                 ub[:], start=(dc == 0), stop=(dc == NDI - 1),
                                 inc=ts)
                            p.op("tensor", "matmul", ps_c16[:],
                                 wx_sb[:, dc * 96 + 80:dc * 96 + 96],
                                 ub[:], start=(dc == 0), stop=(dc == NDI - 1),
                                 inc=ts)
                            p.wait("gpsimd", ss)
                            p.op("gpsimd", "dma_start",
                                 out=u_scr[dc * 128:(dc + 1) * 128, :], in_=ub[:],
                                 inc=st, inck=16)
                            spill_marks[("u", dc % 4)] = st[1]
                        else:
                            dcz = (mg - 4) * 4 + m
                            zb = z_t[dcz % 4]
                            p.wait("scalar", st, spill_marks.get(("z", dcz % 4), 0))
                            p.op("scalar", "activation", out=zb[:], in_=ps_m[m][:],
                                 func=AF.Silu, inc=ss)
                            p.wait("gpsimd", ss)
                            p.op("gpsimd", "dma_start",
                                 out=z_scr[dcz * 128:(dcz + 1) * 128, :], in_=zb[:],
                                 inc=st, inck=16)
                            spill_marks[("z", dcz % 4)] = st[1]
                    p.barrier(("vector", "scalar", "tensor", "sync"))

                p.wait("scalar", ts)
                p.op("scalar", "copy", out=dtraw_bf[:], in_=ps_xd[0:64, :], inc=ss)
                p.wait("vector", ts)
                p.opd("vector", "tensor_copy", out=bc_rows[:], in_=ps_xd[64:80, :], inc=vs)
                p.opd("vector", "tensor_copy", out=cc_rows[:], in_=ps_c16[:], inc=vs)
                p.barrier(("vector", "scalar", "tensor", "sync", "gpsimd"))

                bc_marks = []
                for si in range(NS):
                    for wi2, (rows, dstt) in enumerate(((bc_rows, Bint), (cc_rows, Cint))):
                        it = si * 2 + wi2
                        pb = ps_m[2 + (it % 2)]
                        if it >= 2:
                            p.wait("tensor", vs, bc_marks[it - 2])
                        p.op("tensor", "matmul", pb[:],
                             sel16[:, si * 128:(si + 1) * 128], rows[:],
                             start=True, stop=True, inc=ts)
                        p.wait("vector", ts)
                        p.op("vector", "tensor_copy",
                             out=dstt[:, si * TC:(si + 1) * TC],
                             in_=pb[:], inc=vs)
                        bc_marks.append(vs[1])
                p.barrier(("vector", "tensor", "gpsimd"))

                for dc in range(NDI):
                    pm = ps_m[dc % 2]
                    p.wait("tensor", ss)
                    p.wait("tensor", vs)
                    p.op("tensor", "matmul", pm[:],
                         wdt_bf[:, dc * 128:(dc + 1) * 128], dtraw_bf[:],
                         start=True, stop=True, inc=ts)
                    bcol = bdtc[:, l * 16 + dc:l * 16 + dc + 1]
                    dtb = dt_t[dc % 4]
                    p.wait("scalar", st, spill_marks.get(("dt", dc % 4), 0))
                    p.wait("scalar", ts)
                    p.op("scalar", "activation", out=dtb[:], in_=pm[:],
                          func=AF.Relu, bias=bcol, inc=ss)
                    p.op("scalar", "activation", out=tmp_f[:], in_=pm[:],
                          func=AF.Abs, bias=bcol, inc=ss)
                    p.op("scalar", "activation", out=tmp_f[:], in_=tmp_f[:],
                          func=AF.Exp, scale=-1.0, inc=ss)
                    p.op("scalar", "activation", out=tmp_f[:], in_=tmp_f[:],
                          func=AF.Ln, bias=1.0, inc=ss)
                    p.wait("vector", ss)
                    p.op("vector", "tensor_add", out=dtb[:], in0=dtb[:], in1=tmp_f[:], inc=vs)
                    ub = u_t[dc % 4]
                    p.wait("sync", st, spill_marks.get(("u", dc % 4), 0))
                    p.wait("sync", vs, spill_marks.get(("uv", dc % 4), 0))
                    p.dma("sync", ub[:], u_scr[dc * 128:(dc + 1) * 128, :], ld)
                    p.wait("vector", ld)
                    wb = w_t[dc % 4]
                    p.wait("vector", st, spill_marks.get(("w", dc % 4), 0))
                    p.op("vector", "tensor_mul", out=wb[:], in0=dtb[:], in1=ub[:], inc=vs)
                    spill_marks[("uv", dc % 4)] = vs[1]
                    p.wait("gpsimd", vs)
                    p.op("gpsimd", "dma_start", out=dt_scr[dc * 128:(dc + 1) * 128, :],
                         in_=dtb[:], inc=st, inck=16)
                    p.op("gpsimd", "dma_start", out=w_scr[dc * 128:(dc + 1) * 128, :],
                         in_=wb[:], inc=st, inck=16)
                    spill_marks[("dt", dc % 4)] = st[1]
                    spill_marks[("w", dc % 4)] = st[1]
                p.barrier(("vector", "scalar", "tensor", "sync", "gpsimd"))

                da_marks = {}
                xh_marks = {}
                sdt_marks = {}
                epi_marks = {}
                pq_marks = {}
                dtw_lmarks = {}
                uz_lmarks = {}

                def load_dtw(dc):
                    p.wait("sync", ss, sdt_marks.get(("dt", dc % 4), 0))
                    p.wait("sync", vs, sdt_marks.get(("w", dc % 4), 0))
                    p.dma("sync", dt_t[dc % 4][:], dt_scr[dc * 128:(dc + 1) * 128, :], ld)
                    p.dma("sync", w_t[dc % 4][:], w_scr[dc * 128:(dc + 1) * 128, :], ld)
                    dtw_lmarks[dc] = ld[1]

                def load_uz(dc):
                    p.wait("sync", vs, epi_marks.get(("u", dc % 4), 0))
                    p.dma("sync", u_t[dc % 4][:], u_scr[dc * 128:(dc + 1) * 128, :], ld)
                    p.dma("sync", z_t[dc % 4][:], z_scr[dc * 128:(dc + 1) * 128, :], ld)
                    uz_lmarks[dc] = ld[1]

                load_dtw(0)
                load_dtw(1)
                load_uz(0)
                for dc in range(NDI):
                    if dc + 2 < NDI:
                        load_dtw(dc + 2)
                    if dc + 1 < NDI:
                        load_uz(dc + 1)
                    dtb = dt_t[dc % 4]
                    wb = w_t[dc % 4]
                    p.wait("scalar", ld, dtw_lmarks[dc])
                    p.wait("vector", ld, dtw_lmarks[dc])
                    hc_sl = hcarry[:, (l * NDI + dc) * 16:(l * NDI + dc) * 16 + 16]
                    for q in range(8):
                        it = dc * 8 + q
                        DAb = DA_bufs[it % 2]
                        XHb = XH[it % 2]
                        DA3 = DAb[:].rearrange("p (s c) -> p s c", c=513)
                        XH3 = XHb[:].rearrange("p (s c) -> p s c", c=513)
                        p.wait("scalar", vs, da_marks.get(it % 2, 0))
                        for sq in range(2):
                            si = q * 2 + sq
                            p.op("scalar", "activation",
                                 out=DAb[:, sq * 513 + 1:sq * 513 + 513], in_=dtb[:],
                                 func=AF.Exp,
                                 scale=A_sb[:, l * 256 + dc * 16 + si:l * 256 + dc * 16 + si + 1],
                                 inc=ss)
                        if q == 7:
                            sdt_marks[("dt", dc % 4)] = ss[1]
                        bq = Bint[:, q * 2 * TC:(q + 1) * 2 * TC].rearrange(
                            "p (s c) -> p s c", c=TC)
                        wbb = wb[:].unsqueeze(1).broadcast_to([128, 2, TC])
                        p.wait("vector", ld2, xh_marks.get(it % 2, 0))
                        p.op("vector", "tensor_copy", out=XH3[:, :, 0],
                             in_=hc_sl[:, q * 2:(q + 1) * 2], inc=vs)
                        p.op("vector", "memset", DA3[:, :, 0], 0.0, inc=vs)
                        p.wait("vector", ss)
                        p.op("vector", "tensor_tensor", out=XH3[:, :, 1:513],
                             in0=bq, in1=wbb, op=ALU.mult, inc=vs)
                        if q == 7:
                            sdt_marks[("w", dc % 4)] = vs[1]
                        p.op("vector", "tensor_tensor_scan", out=XHb[:],
                             data0=DAb[:], data1=XHb[:], initial=0.0,
                             op0=ALU.mult, op1=ALU.add, inc=vs)
                        da_marks[it % 2] = vs[1]
                        PQ0 = PQ0b[it % 2]
                        cq = Cint[:, q * 2 * TC:(q + 1) * 2 * TC]
                        if q == 0:
                            p.op("vector", "tensor_mul", out=y0[:],
                                 in0=XHb[:, 1:513], in1=cq[:, 0:TC], inc=vs)
                        else:
                            p.op("vector", "tensor_mul", out=PQ0[:],
                                 in0=XHb[:, 1:513], in1=cq[:, 0:TC], inc=vs)
                            p.op("vector", "tensor_add", out=y0[:],
                                 in0=y0[:], in1=PQ0[:], inc=vs)
                        p.wait("gpsimd", vs)
                        if q == 0:
                            p.op("gpsimd", "tensor_mul", out=y_dc[:],
                                 in0=XHb[:, 514:1026], in1=cq[:, TC:2 * TC], inc=ld2)
                        else:
                            p.op("gpsimd", "tensor_mul", out=PQ1[:],
                                 in0=XHb[:, 514:1026], in1=cq[:, TC:2 * TC], inc=ld2)
                            p.op("gpsimd", "tensor_add", out=y_dc[:],
                                 in0=y_dc[:], in1=PQ1[:], inc=ld2)
                        p.op("gpsimd", "tensor_copy",
                             out=hc_sl[:, q * 2:(q + 1) * 2],
                             in_=XH3[:, :, 512], inc=ld2)
                        xh_marks[it % 2] = ld2[1]
                    ub = u_t[dc % 4]
                    zb = z_t[dc % 4]
                    p.wait("vector", ld, uz_lmarks[dc])
                    p.wait("vector", ld2)
                    p.op("vector", "scalar_tensor_tensor", out=tmp_f[:], in0=ub[:],
                         scalar=dpc[:, l * 16 + dc:l * 16 + dc + 1], in1=y_dc[:],
                         op0=ALU.mult, op1=ALU.add, inc=vs)
                    p.op("vector", "tensor_add", out=tmp_f[:], in0=tmp_f[:],
                         in1=y0[:], inc=vs)
                    p.op("vector", "tensor_mul",
                         out=y2_bf[:, dc * TC:(dc + 1) * TC],
                         in0=tmp_f[:], in1=zb[:], inc=vs)
                    epi_marks[("u", dc % 4)] = vs[1]
                p.barrier(("vector", "scalar", "tensor", "sync", "gpsimd"))

                for og in range(2):
                    def load_wo(i, b, og=og, l=l):
                        kk = i
                        p.dma("sync", wi_bf[b][:],
                              wos[l * DI + kk * 128:l * DI + (kk + 1) * 128,
                                  og * 512:(og + 1) * 512], ld)
                    def use_wo(i, b, og=og):
                        kk = i
                        p.wait("tensor", ld)
                        for m in range(4):
                            inc2 = wdone if m == 3 else (
                                ts if kk == NDI - 1 else None)
                            p.op("tensor", "matmul", ps_m[m][:],
                                 wi_bf[b][:, m * 128:(m + 1) * 128],
                                 y2_bf[:, kk * TC:(kk + 1) * TC],
                                 start=(kk == 0), stop=(kk == NDI - 1),
                                 inc=inc2)
                    stream(p, NDI, 3, load_wo, use_wo, wdone)
                    p.wait("vector", ts)
                    p.wait("vector", wdone)
                    for m in range(4):
                        oc = og * 4 + m
                        p.op("vector", "tensor_add",
                             out=x_cur[:, oc * TC:(oc + 1) * TC],
                             in0=ps_m[m][:],
                             in1=x_cur[:, oc * TC:(oc + 1) * TC], inc=vs)
                    p.barrier(("vector", "tensor", "sync"))
                p.barrier()

            for c in range(NDC):
                p.wait("sync", vs)
                p.dma("sync", tmp_f[:], xT[c * 128:(c + 1) * 128, t0:t0 + TC], ld)
                p.wait("vector", ld)
                p.op("vector", "tensor_add", out=u_t[0][:], in0=tmp_f[:],
                     in1=x_cur[:, c * TC:(c + 1) * TC], inc=vs)
                p.wait("gpsimd", vs)
                p.op("gpsimd", "dma_start", out=oT[c * 128:(c + 1) * 128, t0:t0 + TC],
                     in_=u_t[0][:], inc=st, inck=16)
                p.wait("vector", st)
            p.barrier()


def build_l1attn():
    nc = bass.Bass("TRN2", target_bir_lowering=False)
    DI, NL, TC, DTR = 2048, 4, 512, 64

    xT = nc.dram_tensor("xT", [D, T], F32, kind="ExternalInput").ap()
    WiT = nc.dram_tensor("a3_WiT", [NL, D, 2 * DI], F32, kind="ExternalInput").ap()
    WxT = nc.dram_tensor("a3_WxT", [NL, DI, 96], F32, kind="ExternalInput").ap()
    WdtT = nc.dram_tensor("a3_WdtT", [NL, DTR, DI], F32, kind="ExternalInput").ap()
    WoT = nc.dram_tensor("a3_WoT", [NL, DI, D], F32, kind="ExternalInput").ap()
    cwc_d = nc.dram_tensor("a3_cwc", [NL, 128, 64], F32, kind="ExternalInput").ap()
    cbc_d = nc.dram_tensor("a3_cbc", [NL, 128, 16], F32, kind="ExternalInput").ap()
    bdtc_d = nc.dram_tensor("a3_bdtc", [NL, 128, 16], F32, kind="ExternalInput").ap()
    dpc_d = nc.dram_tensor("a3_dpc", [NL, 128, 16], F32, kind="ExternalInput").ap()
    alogc_d = nc.dram_tensor("a3_alogc", [NL, 128, 256], F32, kind="ExternalInput").ap()
    lnwc_d = nc.dram_tensor("a3_lnwc", [NL, 128, 8], F32, kind="ExternalInput").ap()
    lnbc_d = nc.dram_tensor("a3_lnbc", [NL, 128, 8], F32, kind="ExternalInput").ap()
    cpool_d = nc.dram_tensor("cpool", [128, 8], F32, kind="ExternalInput").ap()
    ones_d = nc.dram_tensor("ones", [128, 128], F32, kind="ExternalInput").ap()
    oT = nc.dram_tensor("oT", [D, T], F32, kind="ExternalOutput").ap()

    z_scr = nc.dram_tensor("a3_z_scr", [DI, TC], F32, kind="Internal").ap()
    u_scr = nc.dram_tensor("a3_u_scr", [DI, TC], F32, kind="Internal").ap()
    dt_scr = nc.dram_tensor("a3_dt_scr", [DI, TC], F32, kind="Internal").ap()
    w_scr = nc.dram_tensor("a3_w_scr", [DI, TC], F32, kind="Internal").ap()
    sel16_d = nc.dram_tensor("a3_sel16", [16, 16 * 128], F32, kind="ExternalInput").ap()
    wis_scr = nc.dram_tensor("a3_wis", [4 * D, 2 * DI], BF16, kind="Internal").ap()
    wos_scr = nc.dram_tensor("a3_wos", [4 * DI, D], BF16, kind="Internal").ap()

    es = ExitStack()
    with es:
        p = Plan(nc, es)
        C = make_ctx(p)
        p.dma("sync", C["cpool"][:], cpool_d, C["ld"])
        p.dma("sync", C["ones"][:], ones_d, C["ld"])
        for e in ("vector", "scalar", "tensor"):
            p.wait(e, C["ld"])
        p.begin_phase("a3_")
        emit_l1attn(p, C, xT, WiT, WxT, WdtT, WoT, cwc_d, cbc_d, bdtc_d, dpc_d,
                    alogc_d, lnwc_d, lnbc_d, sel16_d, (z_scr, u_scr, dt_scr, w_scr),
                    (wis_scr, wos_scr), oT)
        p.end_phase()
        p.wait("gpsimd", C["st"])
        p.wait("sync", C["st"])
        p.finalize()
    return nc


def _sel16():
    m = np.zeros((16, 16 * 128), np.float32)
    for k in range(16):
        m[k, k * 128:(k + 1) * 128] = 1.0
    return m


def prep_l1attn_inputs(x_fm, W_in, conv_w, conv_b, W_x, W_dt, b_dt, A_log, Dp,
                       W_out, ln_w, ln_b):
    NL = 4
    cwc = np.ascontiguousarray(
        conv_w.reshape(NL, 16, 128, 4).transpose(0, 3, 1, 2)
        .reshape(NL, 64, 128).transpose(0, 2, 1))
    cbc = np.ascontiguousarray(conv_b.reshape(NL, 16, 128).transpose(0, 2, 1))
    bdtc = np.ascontiguousarray(b_dt.reshape(NL, 16, 128).transpose(0, 2, 1))
    dpc = np.ascontiguousarray(Dp.reshape(NL, 16, 128).transpose(0, 2, 1))
    alogc = np.ascontiguousarray(
        A_log.reshape(NL, 16, 128, 16).transpose(0, 2, 1, 3).reshape(NL, 128, 256))
    lnwc = np.ascontiguousarray(ln_w.reshape(NL, 8, 128).transpose(0, 2, 1))
    lnbc = np.ascontiguousarray(ln_b.reshape(NL, 8, 128).transpose(0, 2, 1))
    return {
        "xT": np.ascontiguousarray(x_fm),
        "a3_WiT": np.ascontiguousarray(W_in.transpose(0, 2, 1)),
        "a3_WxT": np.ascontiguousarray(W_x.transpose(0, 2, 1)),
        "a3_WdtT": np.ascontiguousarray(W_dt.transpose(0, 2, 1)),
        "a3_WoT": np.ascontiguousarray(W_out.transpose(0, 2, 1)),
        "a3_cwc": cwc, "a3_cbc": cbc, "a3_bdtc": bdtc, "a3_dpc": dpc,
        "a3_alogc": alogc, "a3_lnwc": lnwc, "a3_lnbc": lnbc,
        "a3_sel16": _sel16(),
        "cpool": make_cpool(),
        "ones": np.ones((128, 128), np.float32),
    }



def build_fused():
    nc = bass.Bass("TRN2", target_bir_lowering=False)
    DI, NL, TC, DTR = 2048, 4, 512, 64
    xT = nc.dram_tensor("xT", [D, T], F32, kind="ExternalInput").ap()
    cpool_d = nc.dram_tensor("cpool", [128, 8], F32, kind="ExternalInput").ap()
    ones_d = nc.dram_tensor("ones", [128, 128], F32, kind="ExternalInput").ap()
    ident_d = nc.dram_tensor("ident", [128, 128], F32, kind="ExternalInput").ap()
    oT = nc.dram_tensor("oT", [D, T], F32, kind="ExternalOutput").ap()
    wT = {}
    for nm in ("wq", "wk", "wv", "wg", "wo"):
        wT[nm] = nc.dram_tensor("a1_" + nm + "T", [D, D], F32, kind="ExternalInput").ap()
    ld0 = nc.dram_tensor("a1_ld0", [1, 1], F32, kind="ExternalInput").ap()
    diff_d = nc.dram_tensor("a1_diffBD", [128, 128], F32, kind="ExternalInput").ap()
    tri_d = nc.dram_tensor("a1_triBD", [128, 128], F32, kind="ExternalInput").ap()
    W1 = nc.dram_tensor("a2_W1", [8, D, 4096], F32, kind="ExternalInput").ap()
    W2 = nc.dram_tensor("a2_W2", [8, 4096, D], F32, kind="ExternalInput").ap()
    b1c = nc.dram_tensor("a2_b1c", [128, 256], F32, kind="ExternalInput").ap()
    b2_d = nc.dram_tensor("a2_b2", [8, D], F32, kind="ExternalInput").ap()
    wg_d = nc.dram_tensor("a2_wg", [D, 8], F32, kind="ExternalInput").ap()
    bg_d = nc.dram_tensor("a2_bg", [1, 8], F32, kind="ExternalInput").ap()
    sel8_d = nc.dram_tensor("a2_sel8", [8, 8 * 128], F32, kind="ExternalInput").ap()
    WiT = nc.dram_tensor("a3_WiT", [NL, D, 2 * DI], F32, kind="ExternalInput").ap()
    WxT = nc.dram_tensor("a3_WxT", [NL, DI, 96], F32, kind="ExternalInput").ap()
    WdtT = nc.dram_tensor("a3_WdtT", [NL, DTR, DI], F32, kind="ExternalInput").ap()
    WoT = nc.dram_tensor("a3_WoT", [NL, DI, D], F32, kind="ExternalInput").ap()
    cwc_d = nc.dram_tensor("a3_cwc", [NL, 128, 64], F32, kind="ExternalInput").ap()
    cbc_d = nc.dram_tensor("a3_cbc", [NL, 128, 16], F32, kind="ExternalInput").ap()
    bdtc_d = nc.dram_tensor("a3_bdtc", [NL, 128, 16], F32, kind="ExternalInput").ap()
    dpc_d = nc.dram_tensor("a3_dpc", [NL, 128, 16], F32, kind="ExternalInput").ap()
    alogc_d = nc.dram_tensor("a3_alogc", [NL, 128, 256], F32, kind="ExternalInput").ap()
    lnwc_d = nc.dram_tensor("a3_lnwc", [NL, 128, 8], F32, kind="ExternalInput").ap()
    lnbc_d = nc.dram_tensor("a3_lnbc", [NL, 128, 8], F32, kind="ExternalInput").ap()
    wgT4 = nc.dram_tensor("f4_wgT", [D, 3072], I32, kind="ExternalInput").ap()
    wuT4 = nc.dram_tensor("f4_wuT", [D, 3072], I32, kind="ExternalInput").ap()
    wdT4 = nc.dram_tensor("f4_wdT", [3072, D], I32, kind="ExternalInput").ap()
    act1 = nc.dram_tensor("act1", [D, T], F32, kind="Internal").ap()
    act2 = nc.dram_tensor("act2", [D, T], F32, kind="Internal").ap()
    act3 = nc.dram_tensor("act3", [D, T], F32, kind="Internal").ap()
    z_scr = nc.dram_tensor("a3_z_scr", [DI, TC], F32, kind="Internal").ap()
    u_scr = nc.dram_tensor("a3_u_scr", [DI, TC], F32, kind="Internal").ap()
    dt_scr = nc.dram_tensor("a3_dt_scr", [DI, TC], F32, kind="Internal").ap()
    w_scr = nc.dram_tensor("a3_w_scr", [DI, TC], F32, kind="Internal").ap()
    sel16_d = nc.dram_tensor("a3_sel16", [16, 16 * 128], F32, kind="ExternalInput").ap()
    w1s_scr = nc.dram_tensor("a2_w1s", [8 * D, 4096], BF16, kind="Internal").ap()
    w2s_scr = nc.dram_tensor("a2_w2s", [8 * 4096, D], BF16, kind="Internal").ap()
    wis_scr = nc.dram_tensor("a3_wis", [4 * D, 2 * DI], BF16, kind="Internal").ap()
    wos_scr = nc.dram_tensor("a3_wos", [4 * DI, D], BF16, kind="Internal").ap()
    wgs = nc.dram_tensor("f4_wgs", [D, 3072], BF16, kind="Internal").ap()
    wus = nc.dram_tensor("f4_wus", [D, 3072], BF16, kind="Internal").ap()
    wds = nc.dram_tensor("f4_wds", [3072, D], BF16, kind="Internal").ap()

    es = ExitStack()
    with es:
        p = Plan(nc, es)
        C = make_ctx(p)
        p.dma("sync", C["cpool"][:], cpool_d, C["ld"])
        p.dma("sync", C["ones"][:], ones_d, C["ld"])
        p.dma("sync", C["ident"][:], ident_d, C["ld"])
        for e in ("vector", "scalar", "tensor"):
            p.wait(e, C["ld"])

        def joint():
            p.barrier()
            p.wait("sync", C["st"])
            p.wait("gpsimd", C["st"])

        p.begin_phase("a1_")
        emit_l0attn(p, C, xT, wT, ld0, diff_d, tri_d, act1)
        p.end_phase()
        joint()
        p.begin_phase("a2_")
        emit_l0ffn(p, C, act1, W1, W2, b1c, b2_d, wg_d, bg_d, sel8_d,
                   (w1s_scr, w2s_scr), act2)
        p.end_phase()
        joint()
        p.begin_phase("a3_")
        emit_l1attn(p, C, act2, WiT, WxT, WdtT, WoT, cwc_d, cbc_d, bdtc_d,
                    dpc_d, alogc_d, lnwc_d, lnbc_d, sel16_d,
                    (z_scr, u_scr, dt_scr, w_scr), (wis_scr, wos_scr), act3)
        p.end_phase()
        joint()
        p.begin_phase("f4_")
        emit_l1ffn(p, C, act3, wgT4, wuT4, wdT4, (wgs, wus, wds), oT)
        p.end_phase()
        p.wait("gpsimd", C["st"])
        p.wait("sync", C["st"])
        p.finalize()
    return nc


def prep_fused_inputs(inputs, x_fm):
    m = {}
    m.update(prep_l0attn_inputs(
        x_fm, inputs["L0_attn_W_q"], inputs["L0_attn_W_k"], inputs["L0_attn_W_v"],
        inputs["L0_attn_W_g"], inputs["L0_attn_W_o"],
        np.asarray(inputs["L0_attn_log_decay"])))
    m.update(prep_l0ffn_inputs(
        x_fm, inputs["L0_ffn_W1"], inputs["L0_ffn_b1"], inputs["L0_ffn_W2"],
        inputs["L0_ffn_b2"], inputs["L0_ffn_Wg"], np.asarray(inputs["L0_ffn_bg"])))
    m.update(prep_l1attn_inputs(
        x_fm, inputs["L1_attn_W_in"], inputs["L1_attn_conv_w"],
        inputs["L1_attn_conv_b"], inputs["L1_attn_W_x"], inputs["L1_attn_W_dt"],
        inputs["L1_attn_b_dt"], inputs["L1_attn_A_log"], inputs["L1_attn_Dp"],
        inputs["L1_attn_W_out"], inputs["L1_attn_ln_w"], inputs["L1_attn_ln_b"]))
    m.update(prep_l1ffn_inputs(
        x_fm, inputs["L1_ffn_w_gate"], inputs["L1_ffn_w_up"],
        inputs["L1_ffn_w_down"]))
    return m


def kernel(**inputs):
    x = np.asarray(inputs["L0_attn_x"])
    nc = build_fused()
    base = prep_fused_inputs(inputs, np.ascontiguousarray(x[0].T))
    maps = []
    for b in range(NCORES):
        m = dict(base)
        m["xT"] = np.ascontiguousarray(x[b].T)
        maps.append(m)
    res = run_bass_kernel_spmd(nc, maps, core_ids=list(range(NCORES)))
    out = np.stack([np.asarray(res.results[b]["oT"]).T for b in range(NCORES)])
    return out.astype(np.float32)
